# version 01
# baseline (speedup 1.0000x reference)
import numpy as np
import ml_dtypes
import concourse.bass as bass
import concourse.mybir as mybir
from concourse.bass_utils import run_bass_kernel_spmd
from concourse.bass import IndirectOffsetOnAxis as IOA
from contextlib import ExitStack

F32 = mybir.dt.float32
BF16 = mybir.dt.bfloat16
U32 = mybir.dt.uint32
AF = mybir.ActivationFunctionType
ALU = mybir.AluOpType
AX = mybir.AxisListType
BF = ml_dtypes.bfloat16

NCORES = 8
B, T, D = 4, 2048, 2048
H, DH, DR, DL = 16, 128, 64, 512
NEXP, NZ, TOPK, CAP = 32, 32, 8, 1024
MOE_SCALE = 1.5
EPS_LN = 1e-5
SCALE_C = float(1.0 / np.sqrt(DH))
SCALE_R = float(1.0 / np.sqrt(DR))
EXP_SHIFT = -20.0
NEG = -1.0e9

QTOK = 1024
NWIN = 10
NLOC = 12
LTOK = NLOC * 128
DILN = 32
WINDOW, STRIDE, GLOBAL = 512, 64, 128

ENGINES = ("sync", "scalar", "vector", "tensor", "gpsimd")



class _Op:
    __slots__ = ("eng", "fn", "reads", "writes", "is_dma", "seq", "waits", "tag")

    def __init__(self, eng, fn, reads, writes, is_dma, tag=""):
        self.eng, self.fn = eng, fn
        self.reads, self.writes = reads, writes
        self.is_dma = is_dma
        self.seq = 0
        self.waits = []
        self.tag = tag


def _conflict(a, b):
    return a == b or a.startswith(b + ":") or b.startswith(a + ":")


class Prog:
    def __init__(self, nc):
        self.nc = nc
        self.ops = []
        self.last_write = {}
        self.readers = {}
        self.counts = {eng: {"c": 0, "d": 0} for eng in ENGINES}

    def _edge(self, cur, dep):
        if dep is cur:
            return
        cls = "d" if dep.is_dma else "c"
        key = (dep.eng, cls)
        need = dep.seq
        for i, (e, c, v) in enumerate(cur.waits):
            if (e, c) == key:
                if v < need:
                    cur.waits[i] = (e, c, need)
                return
        cur.waits.append((dep.eng, cls, need))

    def emit(self, eng, fn, reads=(), writes=(), is_dma=False, tag=""):
        op = _Op(eng, fn, tuple(reads), tuple(writes), is_dma, tag)
        cls = "d" if is_dma else "c"
        self.counts[eng][cls] += 16 if is_dma else 1
        op.seq = self.counts[eng][cls]
        for r in op.reads:
            for buf, w in list(self.last_write.items()):
                if _conflict(r, buf):
                    self._edge(op, w)
        for wbuf in op.writes:
            for buf, rs in list(self.readers.items()):
                if _conflict(wbuf, buf):
                    for r in rs:
                        if r.eng == op.eng and not r.is_dma and not op.is_dma:
                            continue
                        self._edge(op, r)
            for buf, w in list(self.last_write.items()):
                if _conflict(wbuf, buf):
                    if w.eng == op.eng and not w.is_dma and not op.is_dma:
                        continue
                    self._edge(op, w)
        for wbuf in op.writes:
            for buf in list(self.last_write):
                if buf == wbuf or buf.startswith(wbuf + ":"):
                    del self.last_write[buf]
            for buf in list(self.readers):
                if buf == wbuf or buf.startswith(wbuf + ":"):
                    del self.readers[buf]
            self.last_write[wbuf] = op
        for r in op.reads:
            self.readers.setdefault(r, []).append(op)
        self.ops.append(op)
        return op

    def build(self, es):
        nc = self.nc
        sems = {}
        for eng in ENGINES:
            for cls in ("c", "d"):
                if self.counts[eng][cls]:
                    sems[(eng, cls)] = es.enter_context(
                        nc.semaphore(f"s_{eng}_{cls}"))
        per_eng = {eng: [o for o in self.ops if o.eng == eng] for eng in ENGINES}
        block = es.enter_context(nc.Block())

        def run_ops(eng_obj, eng_name):
            for op in per_eng[eng_name]:
                for (e, c, v) in op.waits:
                    eng_obj.wait_ge(sems[(e, c)], v)
                inst = op.fn(eng_obj)
                cls = "d" if op.is_dma else "c"
                amt = 16 if op.is_dma else 1
                if inst is None:
                    raise RuntimeError(f"op {op.tag} returned None")
                inst.then_inc(sems[(op.eng, cls)], amt)

        @block.sync
        def _(sync):
            run_ops(sync, "sync")
            for (eng, cls), sem in sems.items():
                if not (eng == "sync"):
                    sync.wait_ge(sem, self.counts[eng][cls])
            if ("sync", "d") in sems:
                sync.wait_ge(sems[("sync", "d")], self.counts["sync"]["d"])

        @block.scalar
        def _(scalar):
            run_ops(scalar, "scalar")

        @block.vector
        def _(vector):
            run_ops(vector, "vector")

        @block.tensor
        def _(tensor):
            run_ops(tensor, "tensor")

        @block.gpsimd
        def _(gpsimd):
            run_ops(gpsimd, "gpsimd")

        return block



def _yarn_tables():
    BASE, MAX_SEQ, ORIG_MAX = 10000.0, 131072, 4096
    BETA_FAST, BETA_SLOW = 32.0, 1.0
    base_inv = 1.0 / BASE ** (np.arange(0, DR, 2, dtype=np.float32) / DR)
    scale = MAX_SEQ / ORIG_MAX
    freqs = np.arange(DR // 2, dtype=np.float32)
    ramp = np.clip((freqs - BETA_SLOW) / (BETA_FAST - BETA_SLOW), 0.0, 1.0)
    inv_freq = base_inv * (1 - ramp) + base_inv / scale * ramp
    yarn = np.float32(0.1 * np.log(MAX_SEQ / ORIG_MAX) + 1.0)
    return inv_freq.astype(np.float32), yarn


def _rope_cos_sin(positions, mult):
    inv_freq, yarn = _yarn_tables()
    t = positions.astype(np.float32)
    fr = t[None, :] * inv_freq[:, None]
    emb = np.concatenate([fr, fr], axis=0)
    cos = np.cos(emb) * (yarn * mult)
    sin = np.sin(emb) * (yarn * mult)
    return cos.astype(BF), sin.astype(BF)


def _ref_mask(qs, ks):
    q = qs[:, None]
    k = ks[None, :]
    dist = q - k
    window = (dist >= -(WINDOW // 2)) & (dist <= WINDOW // 2)
    dilated = (k % STRIDE == 0) | (k < GLOBAL)
    causal = k <= q
    return (window | dilated) & causal


def _core_meta(c):
    b, half = c // 2, c % 2
    q0 = half * QTOK
    dil_ids = np.arange(2, 2 + DILN) * STRIDE
    dil_valid = dil_ids < (q0 + 8 * 128 - 256 - 128 + 128)
    return b, half, q0, dil_ids, dil_valid


def _build_masks(c):
    _, _, q0, dil_ids, _ = _core_meta(c)
    win = np.full((8, 3, 128, 128), NEG, np.float32)
    glob = np.full((8, 128, 128), NEG, np.float32)
    dil = np.full((8, DILN, 128), NEG, np.float32)
    for j in range(8):
        qs = q0 + j * 128 + np.arange(128)
        claimed = set()
        for w in range(3):
            gchunk = q0 // 128 + j - 2 + w
            if gchunk < 0:
                continue
            claimed.add(gchunk)
            ks = gchunk * 128 + np.arange(128)
            m = _ref_mask(qs, ks)
            win[j, w] = np.where(m.T, 0.0, NEG)
        if 0 not in claimed:
            m = _ref_mask(qs, np.arange(128))
            glob[j] = np.where(m.T, 0.0, NEG)
        for s in range(DILN):
            k = dil_ids[s]
            if k // 128 in claimed or k < GLOBAL or k >= T:
                continue
            m = _ref_mask(qs, np.array([k]))[:, 0]
            dil[j, s] = np.where(m, 0.0, NEG)
    return win.astype(BF), glob.astype(BF), dil.astype(BF)


def _perm_rope_lhsT():
    P = np.zeros((64, 64), np.float32)
    for i in range(32):
        P[i, 32 + i] = -1.0
        P[32 + i, i] = 1.0
    P2 = np.zeros((128, 128), np.float32)
    P2[:64, :64] = P
    P2[64:, 64:] = P
    return np.ascontiguousarray(P2.T).astype(BF)


def _l1_inputs(core, x, wq, wdkv, wuk, wuv, wqp, wkp, wo, wcls, bias):
    b, half, q0, dil_ids, _ = _core_meta(core)
    xb = x[b]
    win_rows = np.zeros((NWIN * 128, D), np.float32)
    lo = q0 - 256
    src_lo, dst_lo = max(lo, 0), max(lo, 0) - lo
    win_rows[dst_lo:dst_lo + (q0 + QTOK - src_lo)] = xb[src_lo:q0 + QTOK]
    glob_rows = xb[0:128]
    dil_rows = xb[np.minimum(dil_ids, T - 1)]
    pad = np.zeros((96, D), np.float32)
    x_local = np.ascontiguousarray(
        np.concatenate([win_rows, glob_rows, dil_rows, pad], 0))

    loc_pos = np.zeros(LTOK, np.float32)
    loc_pos[:NWIN * 128] = np.arange(lo, lo + NWIN * 128)
    loc_pos[NWIN * 128:NWIN * 128 + 128] = np.arange(128)
    loc_pos[NWIN * 128 + 128:NWIN * 128 + 160] = dil_ids
    cos_k, sin_k = _rope_cos_sin(loc_pos, 1.0)
    cos_k = np.concatenate([cos_k, cos_k], 0)
    sin_k = np.concatenate([sin_k, sin_k], 0)
    q_pos = np.arange(q0, q0 + QTOK)
    cos_q, sin_q = _rope_cos_sin(q_pos, SCALE_R)
    cos_q = np.concatenate([cos_q, cos_q], 0)
    sin_q = np.concatenate([sin_q, sin_q], 0)

    win_m, glob_m, dil_m = _build_masks(core)
    return {
        "x_local": x_local,
        "w_q": wq, "w_dkv": wdkv, "w_uk": wuk, "w_uv": wuv,
        "w_qp": wqp, "w_kp": wkp, "w_o": wo,
        "w_clsT": np.ascontiguousarray(wcls.T),
        "biasrep": np.ascontiguousarray(np.broadcast_to(bias[None, :], (128, 64))),
        "constsrep": np.ascontiguousarray(
            np.broadcast_to(np.array([EPS_LN, EXP_SHIFT], np.float32)[None, :],
                            (128, 2))),
        "identb": np.eye(128, dtype=np.float32).astype(BF),
        "permT": _perm_rope_lhsT(),
        "cos_k": cos_k, "sin_k": sin_k, "cos_q": cos_q, "sin_q": sin_q,
        "win_m": np.ascontiguousarray(win_m.transpose(2, 0, 1, 3)).reshape(128, -1),
        "glob_m": np.ascontiguousarray(glob_m.transpose(1, 0, 2)).reshape(128, -1),
        "dil_m": np.ascontiguousarray(dil_m.transpose(1, 0, 2)).reshape(DILN, -1),
    }



def build_l1():
    nc = bass.Bass("TRN2", target_bir_lowering=False)

    def din(name, shape, dt=F32):
        return nc.dram_tensor(name, list(shape), dt, kind="ExternalInput").ap()

    def dout(name, shape, dt=F32):
        return nc.dram_tensor(name, list(shape), dt, kind="ExternalOutput").ap()

    x_local = din("x_local", (LTOK, D))
    w_q = din("w_q", (D, D))
    w_dkv = din("w_dkv", (D, DL))
    w_uk = din("w_uk", (DL, D))
    w_uv = din("w_uv", (DL, D))
    w_qp = din("w_qp", (D, H * DR))
    w_kp = din("w_kp", (D, H * DR))
    w_o = din("w_o", (D, D))
    w_clsT = din("w_clsT", (D, 64))
    biasrep = din("biasrep", (128, 64))
    constsrep = din("constsrep", (128, 2))
    identb = din("identb", (128, 128), BF16)
    permT = din("permT", (128, 128), BF16)
    cos_k = din("cos_k", (128, LTOK), BF16)
    sin_k = din("sin_k", (128, LTOK), BF16)
    cos_q = din("cos_q", (128, QTOK), BF16)
    sin_q = din("sin_q", (128, QTOK), BF16)
    win_m = din("win_m", (128, 24 * 128), BF16)
    glob_m = din("glob_m", (128, 8 * 128), BF16)
    dil_m = din("dil_m", (DILN, 8 * 128), BF16)

    x1_out = dout("x1_out", (QTOK, D))
    ln2_out = dout("ln2_out", (QTOK, D), BF16)
    route_out = dout("route_out", (QTOK, 68))

    es = ExitStack()

    def sb(name, shape, dt=BF16):
        return es.enter_context(nc.sbuf_tensor(name, list(shape), dt))

    def pt(name, shape, dt=F32):
        return es.enter_context(nc.psum_tensor(name, list(shape), dt))

    p = Prog(nc)
    E = p.emit

    ln1_fm = sb("ln1_fm", (128, 16 * LTOK))
    ckv_fm = sb("ckv_fm", (128, 4 * LTOK))
    attnout_fm = sb("attnout_fm", (128, 16 * QTOK))
    ident_s = sb("ident_s", (128, 128))
    perm_s = sb("perm_s", (128, 128))
    cosk_s = sb("cosk_s", (128, LTOK))
    sink_s = sb("sink_s", (128, LTOK))
    cosq_s = sb("cosq_s", (128, QTOK))
    sinq_s = sb("sinq_s", (128, QTOK))
    winm_s = sb("winm_s", (128, 24 * 128))
    globm_s = sb("globm_s", (128, 8 * 128))
    dilm_s = sb("dilm_s", (DILN, 8 * 128))
    biasr_s = sb("biasr_s", (128, 64), F32)
    consts_s = sb("consts_s", (128, 2), F32)
    wcls_s = sb("wcls_s", (128, 16 * 64))

    xt = sb("xt", (128, D), F32)
    stat = sb("stat", (128, 8), F32)
    ln1_tm = sb("ln1_tm", (128, D))

    wdkv_s = sb("wdkv_s", (128, 16 * DL))
    wq_s = sb("wq_s", (128, 16 * 128))
    wqp_s = sb("wqp_s", (128, 16 * 128))
    wkp_s = sb("wkp_s", (128, 16 * 128))
    wuk_s = sb("wuk_s", (128, 4 * 128))
    wuv_s = sb("wuv_s", (128, 4 * 128))

    kc_h = sb("kc_h", (128, LTOK))
    krp_h = sb("krp_h", (128, LTOK))
    kr_raw = sb("kr_raw", (128, LTOK))
    kr_rot = sb("kr_rot", (128, LTOK))
    kr_tmp = sb("kr_tmp", (128, LTOK))
    qc_h = sb("qc_h", (128, QTOK))
    qrp_h = sb("qrp_h", (128, QTOK))
    qr_raw = sb("qr_raw", (128, QTOK))
    qr_rot = sb("qr_rot", (128, QTOK))
    qr_tmp = sb("qr_tmp", (128, QTOK))
    v_aug = sb("v_aug", (128, NLOC * 132))
    mask_w = sb("mask_w", (128, 3 * 128), F32)
    mask_g = sb("mask_g", (128, 128), F32)
    mask_d = sb("mask_d", (DILN, 128), F32)
    attn_w = sb("attn_w", (128, 3 * 128))
    attn_g = sb("attn_g", (128, 128))
    attn_d = sb("attn_d", (DILN, 128))
    ao_tm = sb("ao_tm", (128, DH))
    recip_s = sb("recip_s", (128, 1), F32)
    wo_sb = sb("wo_sb", (128, 512))
    ln2_tm = sb("ln2_tm", (128, D))
    probs_s = sb("probs_s", (128, 64), F32)
    m_s = sb("m_s", (128, 64), F32)
    sel_s = sb("sel_s", (128, 64), F32)
    ismax_s = sb("ismax_s", (128, 64), F32)
    w_sel = sb("w_sel", (128, 64), F32)
    rcol = sb("rcol", (128, 6), F32)
    route_sb = sb("route_sb", (128, 68), F32)

    def x1sl(jj, lo, hi):
        return ln1_fm[:, 2 * (jj * 2048 + lo): 2 * (jj * 2048 + hi)].bitcast(F32)
    wo_s = wdkv_s
    ln2_fmt = ln1_tm

    ps_t = pt("ps_t", (128, 128), BF16)
    ps_a = pt("ps_a", (128, 512))
    ps_b = pt("ps_b", (128, 512))
    ps_s = pt("ps_s", (128, 3 * 128))
    ps_g = pt("ps_g", (128, 128))
    ps_d = pt("ps_d", (DILN, 128))
    ps_o = pt("ps_o", (128, 132))
    ps_r = pt("ps_r", (128, 512))

    def dma(eng, out_ap, in_ap, writes, reads=(), tag=""):
        E(eng, lambda e, o=out_ap, i=in_ap: e.dma_start(out=o, in_=i),
          reads=reads, writes=writes, is_dma=True, tag=tag)

    def mm(out_ap, lhsT_ap, rhs_ap, start, stop, reads, writes, tag=""):
        E("tensor",
          lambda e, o=out_ap, l=lhsT_ap, r=rhs_ap, s1=start, s2=stop:
              e.matmul(out=o, lhsT=l, rhs=r, start=s1, stop=s2),
          reads=reads, writes=writes, tag=tag)

    def transpose(out_ap, in_ap, reads, writes, tag=""):
        E("tensor",
          lambda e, o=out_ap, i=in_ap: e.transpose(out=o, in_=i, identity=ident_s[:]),
          reads=tuple(reads) + ("ident_s",), writes=writes, tag=tag)

    def vcopy(out_ap, in_ap, reads, writes, tag=""):
        E("vector", lambda e, o=out_ap, i=in_ap: e.tensor_copy(out=o, in_=i),
          reads=reads, writes=writes, tag=tag)

    dma("sync", ident_s[:], identb[:], ("ident_s",))
    dma("sync", perm_s[:], permT[:], ("perm_s",))
    dma("sync", cosk_s[:], cos_k[:], ("cosk_s",))
    dma("sync", sink_s[:], sin_k[:], ("sink_s",))
    dma("sync", cosq_s[:], cos_q[:], ("cosq_s",))
    dma("sync", sinq_s[:], sin_q[:], ("sinq_s",))
    dma("sync", winm_s[:], win_m[:], ("winm_s",))
    dma("sync", globm_s[:], glob_m[:], ("globm_s",))
    dma("sync", dilm_s[:], dil_m[:], ("dilm_s",))
    dma("sync", biasr_s[:], biasrep[:], ("biasr_s",))
    dma("sync", consts_s[:], constsrep[:], ("consts_s",))
    E("gpsimd", lambda e: e.memset(v_aug[:], 0.0), writes=("v_aug",), tag="vz")
    for t in range(NLOC):
        E("gpsimd", lambda e, t=t: e.memset(v_aug[:, t * 132 + 128:t * 132 + 129], 1.0),
          reads=(), writes=("v_aug:ones%d" % t,), tag="vone")
    dma("gpsimd", wdkv_s[:].rearrange("p (k n) -> p k n", k=16), w_dkv.rearrange("(k p) n -> p k n", p=128), ("wdkv_s",))
    dma("gpsimd", wcls_s[:].rearrange("p (k n) -> p k n", k=16), w_clsT.rearrange("(k p) n -> p k n", p=128), ("wcls_s",))

    def layer_norm_tile(src_dram_rows, xkey, out_bf16_ap, statk):
        dma("sync", xt[:], src_dram_rows, (xkey,))
        E("vector", lambda e: e.reduce_sum(out=stat[:, 0:1], in_=xt[:], axis=AX.X),
          reads=(xkey,), writes=(statk + ":0",), tag="ln_sum")
        E("scalar", lambda e: e.activation(out=ln1_tm[:], in_=xt[:], func=AF.Square,
                                           accum_out=stat[:, 1:2]),
          reads=(xkey,), writes=("ln1_tm", statk + ":1"), tag="ln_sq")
        E("scalar", lambda e: e.activation(out=stat[:, 2:3], in_=stat[:, 0:1],
                                           func=AF.Copy, scale=1.0 / D),
          reads=(statk + ":0",), writes=(statk + ":2",), tag="ln_mean")
        E("scalar", lambda e: e.activation(out=stat[:, 3:4], in_=stat[:, 1:2],
                                           func=AF.Copy, scale=1.0 / D),
          reads=(statk + ":1",), writes=(statk + ":3",), tag="ln_msq")
        E("vector", lambda e: e.scalar_tensor_tensor(
            out=stat[:, 4:5], in0=stat[:, 2:3], scalar=stat[:, 2:3],
            in1=stat[:, 3:4], op0=ALU.mult, op1=ALU.subtract),
          reads=(statk + ":2", statk + ":3"), writes=(statk + ":4",), tag="ln_negv")
        E("scalar", lambda e: e.activation(out=stat[:, 5:6], in_=stat[:, 4:5],
                                           func=AF.Sqrt, scale=-1.0,
                                           bias=consts_s[:, 0:1]),
          reads=(statk + ":4", "consts_s"), writes=(statk + ":5",), tag="ln_std")
        E("vector", lambda e: e.reciprocal(out=stat[:, 6:7], in_=stat[:, 5:6]),
          reads=(statk + ":5",), writes=(statk + ":6",), tag="ln_rstd")
        E("vector", lambda e, o=out_bf16_ap: e.tensor_scalar(
            out=o, in0=xt[:], scalar1=stat[:, 2:3], scalar2=stat[:, 6:7],
            op0=ALU.subtract, op1=ALU.mult),
          reads=(xkey, statk + ":2", statk + ":6"), writes=("ln1_tm",), tag="ln_norm")

    for i in range(NLOC):
        layer_norm_tile(x_local[i * 128:(i + 1) * 128, :], "xt", ln1_tm[:], "stat")
        for k in range(16):
            transpose(ps_t[:], ln1_tm[:, k * 128:(k + 1) * 128],
                      reads=("ln1_tm",), writes=("ps_t",), tag="t_ln1")
            vcopy(ln1_fm[:, k * LTOK + i * 128: k * LTOK + (i + 1) * 128], ps_t[:],
                  reads=("ps_t",), writes=("ln1_fm:%d_%d" % (k, i),), tag="e_ln1")

    for m in range(4):
        for n in range(3):
            for k in range(16):
                mm(ps_a[:], wdkv_s[:, (k * 4 + m) * 128:(k * 4 + m + 1) * 128],
                   ln1_fm[:, k * LTOK + n * 512: k * LTOK + (n + 1) * 512],
                   start=(k == 0), stop=(k == 15),
                   reads=("wdkv_s", "ln1_fm"), writes=("ps_a",), tag="ckv_mm")
            vcopy(ckv_fm[:, m * LTOK + n * 512: m * LTOK + (n + 1) * 512], ps_a[:],
                  reads=("ps_a",), writes=("ckv_fm:%d_%d" % (m, n),), tag="ckv_ev")

    for h in range(H):
        hp = h // 2
        ho = (h % 2) * 64
        dma("gpsimd", wq_s[:].rearrange("p (k m) -> p k m", k=16),
            w_q[:, h * 128:(h + 1) * 128].rearrange("(k p) m -> p k m", p=128),
            ("wq_s",), tag="ld_wq")
        dma("gpsimd", wuk_s[:].rearrange("p (k m) -> p k m", k=4),
            w_uk[:, h * 128:(h + 1) * 128].rearrange("(k p) m -> p k m", p=128),
            ("wuk_s",), tag="ld_wuk")
        dma("gpsimd", wuv_s[:].rearrange("p (k m) -> p k m", k=4),
            w_uv[:, h * 128:(h + 1) * 128].rearrange("(k p) m -> p k m", p=128),
            ("wuv_s",), tag="ld_wuv")
        if h % 2 == 0:
            dma("gpsimd", wqp_s[:].rearrange("p (k m) -> p k m", k=16),
                w_qp[:, h * 64: h * 64 + 128].rearrange("(k p) m -> p k m", p=128),
                ("wqp_s",), tag="ld_wqp")
            dma("gpsimd", wkp_s[:].rearrange("p (k m) -> p k m", k=16),
                w_kp[:, h * 64: h * 64 + 128].rearrange("(k p) m -> p k m", p=128),
                ("wkp_s",), tag="ld_wkp")

        for n in range(2):
            for k in range(16):
                mm(ps_a[:], wq_s[:, k * 128:(k + 1) * 128],
                   ln1_fm[:, k * LTOK + 256 + n * 512: k * LTOK + 256 + (n + 1) * 512],
                   start=(k == 0), stop=(k == 15),
                   reads=("wq_s", "ln1_fm"), writes=("ps_a",), tag="qc_mm")
            E("vector", lambda e, n=n: e.tensor_scalar(
                out=qc_h[:, n * 512:(n + 1) * 512], in0=ps_a[:], scalar1=SCALE_C,
                scalar2=None, op0=ALU.mult),
              reads=("ps_a",), writes=("qc_h:%d" % n,), tag="qc_ev")

        for n in range(3):
            for m in range(4):
                mm(ps_b[:], wuk_s[:, m * 128:(m + 1) * 128],
                   ckv_fm[:, m * LTOK + n * 512: m * LTOK + (n + 1) * 512],
                   start=(m == 0), stop=(m == 3),
                   reads=("wuk_s", "ckv_fm"), writes=("ps_b",), tag="kc_mm")
            vcopy(kc_h[:, n * 512:(n + 1) * 512], ps_b[:],
                  reads=("ps_b",), writes=("kc_h:%d" % n,), tag="kc_ev")

        for t in range(NLOC):
            for m in range(4):
                mm(ps_b[:, 0:128],
                   ckv_fm[:, m * LTOK + t * 128: m * LTOK + (t + 1) * 128],
                   wuv_s[:, m * 128:(m + 1) * 128],
                   start=(m == 0), stop=(m == 3),
                   reads=("ckv_fm", "wuv_s"), writes=("ps_b",), tag="v_mm")
            vcopy(v_aug[:, t * 132: t * 132 + 128], ps_b[:, 0:128],
                  reads=("ps_b",), writes=("v_aug:%d" % t,), tag="v_ev")

        if h % 2 == 0:
            for n in range(2):
                for k in range(16):
                    mm(ps_a[:], wqp_s[:, k * 128:(k + 1) * 128],
                       ln1_fm[:, k * LTOK + 256 + n * 512: k * LTOK + 256 + (n + 1) * 512],
                       start=(k == 0), stop=(k == 15),
                       reads=("wqp_s", "ln1_fm"), writes=("ps_a",), tag="qr_mm")
                vcopy(qr_raw[:, n * 512:(n + 1) * 512], ps_a[:],
                      reads=("ps_a",), writes=("qr_raw:%d" % n,), tag="qr_ev")
            for n in range(2):
                mm(ps_r[:], perm_s[:], qr_raw[:, n * 512:(n + 1) * 512],
                   start=True, stop=True,
                   reads=("perm_s", "qr_raw"), writes=("ps_r",), tag="qrot_mm")
                vcopy(qr_rot[:, n * 512:(n + 1) * 512], ps_r[:],
                      reads=("ps_r",), writes=("qr_rot:%d" % n,), tag="qrot_ev")
            E("vector", lambda e: e.tensor_tensor(
                out=qr_tmp[:], in0=qr_rot[:], in1=sinq_s[:], op=ALU.mult),
              reads=("qr_rot:0", "qr_rot:1", "sinq_s"), writes=("qr_tmp",), tag="qrope1")
            E("vector", lambda e: e.tensor_tensor(
                out=qrp_h[:], in0=qr_raw[:], in1=cosq_s[:], op=ALU.mult),
              reads=("qr_raw:0", "qr_raw:1", "cosq_s"), writes=("qrp_h:a",), tag="qrope3")
            E("vector", lambda e: e.tensor_tensor(
                out=qrp_h[:], in0=qrp_h[:], in1=qr_tmp[:], op=ALU.add),
              reads=("qrp_h:a", "qr_tmp"),
              writes=("qrp_h",), tag="qrope5")

            for n in range(3):
                for k in range(16):
                    mm(ps_a[:], wkp_s[:, k * 128:(k + 1) * 128],
                       ln1_fm[:, k * LTOK + n * 512: k * LTOK + (n + 1) * 512],
                       start=(k == 0), stop=(k == 15),
                       reads=("wkp_s", "ln1_fm"), writes=("ps_a",), tag="kr_mm")
                vcopy(kr_raw[:, n * 512:(n + 1) * 512], ps_a[:],
                      reads=("ps_a",), writes=("kr_raw:%d" % n,), tag="kr_ev")
            for n in range(3):
                mm(ps_r[:], perm_s[:], kr_raw[:, n * 512:(n + 1) * 512],
                   start=True, stop=True,
                   reads=("perm_s", "kr_raw"), writes=("ps_r",), tag="krot_mm")
                vcopy(kr_rot[:, n * 512:(n + 1) * 512], ps_r[:],
                      reads=("ps_r",), writes=("kr_rot:%d" % n,), tag="krot_ev")
            E("vector", lambda e: e.tensor_tensor(
                out=kr_tmp[:], in0=kr_rot[:], in1=sink_s[:], op=ALU.mult),
              reads=("kr_rot:0", "kr_rot:1", "kr_rot:2", "sink_s"),
              writes=("kr_tmp",), tag="krope1")
            E("vector", lambda e: e.tensor_tensor(
                out=krp_h[:], in0=kr_raw[:], in1=cosk_s[:], op=ALU.mult),
              reads=("kr_raw:0", "kr_raw:1", "kr_raw:2", "cosk_s"),
              writes=("krp_h:a",), tag="krope3")
            E("vector", lambda e: e.tensor_tensor(
                out=krp_h[:], in0=krp_h[:], in1=kr_tmp[:], op=ALU.add),
              reads=("krp_h:a", "kr_tmp"),
              writes=("krp_h",), tag="krope5")

        for j in range(8):
            qsl = slice(j * 128, (j + 1) * 128)
            for w in range(3):
                L = j + w
                mm(ps_s[:, w * 128:(w + 1) * 128],
                   kc_h[:, L * 128:(L + 1) * 128], qc_h[:, qsl],
                   start=True, stop=False,
                   reads=("kc_h:%d" % (L * 128 // 512), "qc_h"), writes=("ps_s",),
                   tag="sc_w")
                mm(ps_s[:, w * 128:(w + 1) * 128],
                   krp_h[ho:ho + 64, L * 128:(L + 1) * 128], qrp_h[ho:ho + 64, qsl],
                   start=False, stop=True,
                   reads=("krp_h", "qrp_h"), writes=("ps_s",), tag="sc_wr")
            mm(ps_g[:], kc_h[:, 10 * 128:11 * 128], qc_h[:, qsl],
               start=True, stop=False, reads=("kc_h:2", "qc_h"), writes=("ps_g",),
               tag="sc_g")
            mm(ps_g[:], krp_h[ho:ho + 64, 10 * 128:11 * 128], qrp_h[ho:ho + 64, qsl],
               start=False, stop=True, reads=("krp_h", "qrp_h"), writes=("ps_g",),
               tag="sc_gr")
            mm(ps_d[:], kc_h[:, 11 * 128:11 * 128 + 32], qc_h[:, qsl],
               start=True, stop=False, reads=("kc_h:2", "qc_h"), writes=("ps_d",),
               tag="sc_d")
            mm(ps_d[:], krp_h[ho:ho + 64, 11 * 128:11 * 128 + 32], qrp_h[ho:ho + 64, qsl],
               start=False, stop=True, reads=("krp_h", "qrp_h"), writes=("ps_d",),
               tag="sc_dr")
            E("vector", lambda e, j=j: e.tensor_tensor(
                out=mask_w[:], in0=ps_s[:], in1=winm_s[:, j * 384:(j + 1) * 384],
                op=ALU.add),
              reads=("ps_s", "winm_s"), writes=("mask_w",), tag="msk_w")
            E("vector", lambda e, j=j: e.tensor_tensor(
                out=mask_g[:], in0=ps_g[:], in1=globm_s[:, j * 128:(j + 1) * 128],
                op=ALU.add),
              reads=("ps_g", "globm_s"), writes=("mask_g",), tag="msk_g")
            E("vector", lambda e, j=j: e.tensor_tensor(
                out=mask_d[:], in0=ps_d[:], in1=dilm_s[:, j * 128:(j + 1) * 128],
                op=ALU.add),
              reads=("ps_d", "dilm_s"), writes=("mask_d",), tag="msk_d")
            E("scalar", lambda e: e.activation(out=attn_w[:], in_=mask_w[:],
                                               func=AF.Exp, bias=consts_s[:, 1:2]),
              reads=("mask_w", "consts_s"), writes=("attn_w",), tag="exp_w")
            E("scalar", lambda e: e.activation(out=attn_g[:], in_=mask_g[:],
                                               func=AF.Exp, bias=consts_s[:, 1:2]),
              reads=("mask_g", "consts_s"), writes=("attn_g",), tag="exp_g")
            E("scalar", lambda e: e.activation(out=attn_d[:], in_=mask_d[:],
                                               func=AF.Exp, bias=consts_s[0:DILN, 1:2]),
              reads=("mask_d", "consts_s"), writes=("attn_d",), tag="exp_d")
            for w in range(3):
                L = j + w
                mm(ps_o[:, 0:129], attn_w[:, w * 128:(w + 1) * 128],
                   v_aug[:, L * 132: L * 132 + 129],
                   start=(w == 0), stop=False,
                   reads=("attn_w", "v_aug:%d" % L, "v_aug:ones%d" % L),
                   writes=("ps_o",), tag="av_w")
            mm(ps_o[:, 0:129], attn_g[:], v_aug[:, 10 * 132: 10 * 132 + 129],
               start=False, stop=False,
               reads=("attn_g", "v_aug:10", "v_aug:ones10"), writes=("ps_o",),
               tag="av_g")
            mm(ps_o[:, 0:129], attn_d[:], v_aug[0:32, 11 * 132: 11 * 132 + 129],
               start=False, stop=True,
               reads=("attn_d", "v_aug:11", "v_aug:ones11"), writes=("ps_o",),
               tag="av_d")
            E("vector", lambda e: e.reciprocal(out=recip_s[:], in_=ps_o[:, 128:129]),
              reads=("ps_o",), writes=("recip_s",), tag="av_rec")
            E("vector", lambda e: e.tensor_scalar(
                out=ao_tm[:], in0=ps_o[:, 0:128], scalar1=recip_s[:, 0:1],
                scalar2=None, op0=ALU.mult),
              reads=("ps_o", "recip_s"), writes=("ao_tm",), tag="av_ev")
            transpose(ps_t[:], ao_tm[:], reads=("ao_tm",), writes=("ps_t",),
                      tag="t_ao")
            vcopy(attnout_fm[:, h * QTOK + j * 128: h * QTOK + (j + 1) * 128],
                  ps_t[:], reads=("ps_t",), writes=("attnout_fm:%d_%d" % (h, j),),
                  tag="e_ao")

    E("vector", lambda e: e.nop(),
      reads=(), writes=("ln1_fm", "x1_half", "wdkv_s", "wo_s", "ln1_tm",
                        "ln2_fmt", "xt"), tag="fence")

    for g in range(2):
        for m in range(16):
            dma("gpsimd", wo_s[:, 0:2048].rearrange("p (k n) -> p k n", k=16),
                w_o[:, m * 128:(m + 1) * 128].rearrange("(k p) n -> p k n", p=128),
                ("wo_s",), tag="ld_wo")
            mmread = ["wo_s"] + ["attnout_fm"]
            for k in range(16):
                mm(ps_a[:], wo_s[:, k * 128:(k + 1) * 128],
                   attnout_fm[:, k * QTOK + g * 512: k * QTOK + (g + 1) * 512],
                   start=(k == 0), stop=(k == 15),
                   reads=tuple(mmread), writes=("ps_a",), tag="wo_mm")
            vcopy(wo_sb[:], ps_a[:], reads=("ps_a",), writes=("wo_sb",), tag="wo_ev")
            for jj in range(4):
                j = g * 4 + jj
                transpose(ps_t[:], wo_sb[:, jj * 128:(jj + 1) * 128],
                          reads=("wo_sb",), writes=("ps_t",), tag="t_wo")
                vcopy(x1sl(jj, m * 128, (m + 1) * 128),
                      ps_t[:], reads=("ps_t",),
                      writes=("x1_half:%d_%d" % (jj, m),), tag="e_wo")
        for jj in range(4):
            j = g * 4 + jj
            dma("sync", xt[:], x_local[256 + j * 128: 256 + (j + 1) * 128, :],
                ("xt",), tag="ld_xres")
            E("vector", lambda e, jj=jj: e.tensor_tensor(
                out=x1sl(jj, 0, 2048), in0=x1sl(jj, 0, 2048), in1=xt[:], op=ALU.add),
              reads=("xt",) + tuple("x1_half:%d_%d" % (jj, m) for m in range(16)),
              writes=("x1_half:%d" % jj,), tag="res_add")
            dma("sync", x1_out[j * 128:(j + 1) * 128, :], x1sl(jj, 0, 2048),
                ("x1_out:%d" % j,), reads=("x1_half:%d" % jj,), tag="st_x1")
            x1v = x1sl(jj, 0, 2048)
            E("vector", lambda e, s=x1v: e.reduce_sum(out=stat[:, 0:1], in_=s, axis=AX.X),
              reads=("x1_half:%d" % jj,), writes=("stat:0",), tag="ln2_sum")
            E("scalar", lambda e, s=x1v: e.activation(out=ln2_tm[:], in_=s,
                                                       func=AF.Square,
                                                       accum_out=stat[:, 1:2]),
              reads=("x1_half:%d" % jj,), writes=("ln2_tm", "stat:1"), tag="ln2_sq")
            E("scalar", lambda e: e.activation(out=stat[:, 2:3], in_=stat[:, 0:1],
                                               func=AF.Copy, scale=1.0 / D),
              reads=("stat:0",), writes=("stat:2",), tag="ln2_mean")
            E("scalar", lambda e: e.activation(out=stat[:, 3:4], in_=stat[:, 1:2],
                                               func=AF.Copy, scale=1.0 / D),
              reads=("stat:1",), writes=("stat:3",), tag="ln2_msq")
            E("vector", lambda e: e.scalar_tensor_tensor(
                out=stat[:, 4:5], in0=stat[:, 2:3], scalar=stat[:, 2:3],
                in1=stat[:, 3:4], op0=ALU.mult, op1=ALU.subtract),
              reads=("stat:2", "stat:3"), writes=("stat:4",), tag="ln2_negv")
            E("scalar", lambda e: e.activation(out=stat[:, 5:6], in_=stat[:, 4:5],
                                               func=AF.Sqrt, scale=-1.0,
                                               bias=consts_s[:, 0:1]),
              reads=("stat:4", "consts_s"), writes=("stat:5",), tag="ln2_std")
            E("vector", lambda e: e.reciprocal(out=stat[:, 6:7], in_=stat[:, 5:6]),
              reads=("stat:5",), writes=("stat:6",), tag="ln2_rstd")
            E("vector", lambda e, s=x1v: e.tensor_scalar(
                out=ln2_tm[:], in0=s, scalar1=stat[:, 2:3], scalar2=stat[:, 6:7],
                op0=ALU.subtract, op1=ALU.mult),
              reads=("x1_half:%d" % jj, "stat:2", "stat:6"), writes=("ln2_tm",),
              tag="ln2_norm")
            dma("sync", ln2_out[j * 128:(j + 1) * 128, :], ln2_tm[:],
                ("ln2_out:%d" % j,), reads=("ln2_tm",), tag="st_ln2")
            for k in range(16):
                transpose(ps_t[:], ln2_tm[:, k * 128:(k + 1) * 128],
                          reads=("ln2_tm",), writes=("ps_t",), tag="t_ln2")
                vcopy(ln2_fmt[:, k * 128:(k + 1) * 128], ps_t[:],
                      reads=("ps_t",), writes=("ln2_fmt:%d" % k,), tag="e_ln2")
            for k in range(16):
                mm(ps_b[:, 0:64], ln2_fmt[:, k * 128:(k + 1) * 128],
                   wcls_s[:, k * 64:(k + 1) * 64],
                   start=(k == 0), stop=(k == 15),
                   reads=("ln2_fmt:%d" % k, "wcls_s"), writes=("ps_b",), tag="lg_mm")
            E("vector", lambda e: e.reduce_max(out=rcol[:, 0:1], in_=ps_b[:, 0:64],
                                               axis=AX.X, negate=True),
              reads=("ps_b",), writes=("rcol:0",), tag="rt_max")
            E("scalar", lambda e: e.activation(out=probs_s[:], in_=ps_b[:, 0:64],
                                               func=AF.Exp, bias=rcol[:, 0:1],
                                               accum_out=rcol[:, 1:2]),
              reads=("ps_b", "rcol:0"), writes=("probs_s", "rcol:1"), tag="rt_exp")
            E("vector", lambda e: e.reciprocal(out=rcol[:, 2:3], in_=rcol[:, 1:2]),
              reads=("rcol:1",), writes=("rcol:2",), tag="rt_rec")
            E("vector", lambda e: e.tensor_scalar(
                out=probs_s[:], in0=probs_s[:], scalar1=rcol[:, 2:3], scalar2=None,
                op0=ALU.mult),
              reads=("probs_s", "rcol:2"), writes=("probs_s",), tag="rt_nrm")
            E("vector", lambda e: e.tensor_tensor(out=m_s[:], in0=probs_s[:],
                                                  in1=biasr_s[:], op=ALU.add),
              reads=("probs_s", "biasr_s"), writes=("m_s",), tag="rt_m0")
            E("gpsimd", lambda e: e.memset(sel_s[:], 0.0), writes=("sel_s",),
              tag="rt_selz")
            for it in range(TOPK):
                E("vector", lambda e: e.reduce_max(out=rcol[:, 3:4], in_=m_s[:],
                                                   axis=AX.X),
                  reads=("m_s",), writes=("rcol:3",), tag="rt_imax")
                E("vector", lambda e: e.tensor_scalar(
                    out=ismax_s[:], in0=m_s[:], scalar1=rcol[:, 3:4], scalar2=None,
                    op0=ALU.is_equal),
                  reads=("m_s", "rcol:3"), writes=("ismax_s",), tag="rt_ismax")
                E("vector", lambda e: e.tensor_tensor(out=sel_s[:], in0=sel_s[:],
                                                      in1=ismax_s[:], op=ALU.add),
                  reads=("sel_s", "ismax_s"), writes=("sel_s",), tag="rt_selacc")
                E("vector", lambda e: e.scalar_tensor_tensor(
                    out=m_s[:], in0=ismax_s[:], scalar=-1.0e30, in1=m_s[:],
                    op0=ALU.mult, op1=ALU.add),
                  reads=("ismax_s", "m_s"), writes=("m_s",), tag="rt_mask")
            E("vector", lambda e: e.tensor_tensor(out=w_sel[:], in0=sel_s[:],
                                                  in1=probs_s[:], op=ALU.mult),
              reads=("sel_s", "probs_s"), writes=("w_sel",), tag="rt_wsel")
            E("vector", lambda e: e.reduce_sum(out=rcol[:, 4:5], in_=w_sel[:],
                                               axis=AX.X),
              reads=("w_sel",), writes=("rcol:4",), tag="rt_wsum")
            E("vector", lambda e: e.tensor_scalar(
                out=rcol[:, 4:5], in0=rcol[:, 4:5], scalar1=1e-20, scalar2=None,
                op0=ALU.add),
              reads=("rcol:4",), writes=("rcol:4",), tag="rt_weps")
            E("vector", lambda e: e.reciprocal(out=rcol[:, 5:6], in_=rcol[:, 4:5]),
              reads=("rcol:4",), writes=("rcol:5",), tag="rt_wrec")
            E("vector", lambda e: e.tensor_scalar(
                out=w_sel[:], in0=w_sel[:], scalar1=rcol[:, 5:6], scalar2=MOE_SCALE,
                op0=ALU.mult, op1=ALU.mult),
              reads=("w_sel", "rcol:5"), writes=("w_sel",), tag="rt_wfin")
            E("vector", lambda e: e.tensor_copy(out=route_sb[:, 0:32],
                                                in_=sel_s[:, 0:32]),
              reads=("sel_s",), writes=("route_sb:a",), tag="rt_o1")
            E("vector", lambda e: e.tensor_copy(out=route_sb[:, 32:64],
                                                in_=w_sel[:, 0:32]),
              reads=("w_sel",), writes=("route_sb:b",), tag="rt_o2")
            E("vector", lambda e: e.reduce_sum(out=route_sb[:, 64:65],
                                               in_=w_sel[:, 32:64], axis=AX.X),
              reads=("w_sel",), writes=("route_sb:c",), tag="rt_o3")
            E("gpsimd", lambda e: e.memset(route_sb[:, 65:68], 0.0),
              writes=("route_sb:d",), tag="rt_o4")
            dma("sync", route_out[j * 128:(j + 1) * 128, :], route_sb[:],
                ("route_out:%d" % j,),
                reads=("route_sb:a", "route_sb:b", "route_sb:c", "route_sb:d"),
                tag="st_route")

    p.build(es)
    es.close()
    return nc





def build_l2():
    nc = bass.Bass("TRN2", target_bir_lowering=False)

    def din(name, shape, dt=F32):
        return nc.dram_tensor(name, list(shape), dt, kind="ExternalInput").ap()

    ln2_all = din("ln2_all", (8192, D), BF16)
    a4sb = din("a4sb", (128, 64 * 4))
    w4sb = din("w4sb", (128, 64 * 4))
    wg4 = din("wg4", (4, 16, D, 128))
    wd4 = din("wd4", (4, 1024, D))
    utri = din("utri", (128, 128), BF16)
    onescol = din("onescol", (128, 1), BF16)
    ones1f = din("ones1f", (1, 128))
    tlocol = din("tlocol", (128, 1), BF16)

    p_out = nc.dram_tensor("p_out", [8192, D], BF16, kind="ExternalOutput").ap()
    bufd = [nc.dram_tensor(f"bufd{e}", [CAP, 2052], BF16).ap() for e in range(4)]

    es = ExitStack()

    def sb(name, shape, dt=BF16):
        return es.enter_context(nc.sbuf_tensor(name, list(shape), dt))

    def pt(name, shape, dt=F32):
        return es.enter_context(nc.psum_tensor(name, list(shape), dt))

    p = Prog(nc)
    E = p.emit

    a_sb = sb("a_sb", (128, 256), F32)
    w_sb = sb("w_sb", (128, 256), F32)
    ut_s = sb("ut_s", (128, 128))
    ones_s = sb("ones_s", (128, 1))
    ones1f_s = sb("ones1f_s", (1, 128), F32)
    tlo_s = sb("tlo_s", (128, 1))
    zero_s = sb("zero_s", (128, 2052))
    stage = sb("stage", (128, 2052))
    tot_sb = sb("tot_sb", (1, 256), F32)
    off_sb = sb("off_sb", (1, 256), F32)
    pos_sb = sb("pos_sb", (128, 4), F32)
    ind_f = sb("ind_f", (128, 4), F32)
    ind_u = sb("ind_u", (128, 4), U32)
    buf_fm = sb("buf_fm", (128, 16 * 1024))
    gu_sb = sb("gu_sb", (128, 16 * 1024))
    silu_sb = sb("silu_sb", (128, 1024))
    act_fm = sb("act_fm", (128, 8 * 1024))
    wd_sb = sb("wd_sb", (128, 8 * 2048))
    wgm = sb("wgm", (128, 16 * 128))
    wt_sb = sb("wt_sb", (128, 8 * 4))
    wcolf = sb("wcolf", (128, 8), F32)
    tcol = sb("tcol", (128, 8), F32)
    oind_f = sb("oind_f", (128, 8), F32)
    oind_u = sb("oind_u", (128, 8), U32)
    yw_sb = sb("yw_sb", (128, 2048))

    _regs = {}

    def get_reg(eng, v):
        if v not in _regs:
            _regs[v] = eng.to_reg(v)
        return _regs[v]

    ps_pos = pt("ps_pos", (128, 4))
    ps_tot = pt("ps_tot", (1, 4))
    ps_gu = pt("ps_gu", (128, 512))
    ps_gu2 = pt("ps_gu2", (128, 512))
    ps_y = pt("ps_y", (128, 512))
    ps_y2 = pt("ps_y2", (128, 512))

    def dma(eng, out_ap, in_ap, writes, reads=(), tag=""):
        E(eng, lambda e, o=out_ap, i=in_ap: e.dma_start(out=o, in_=i),
          reads=reads, writes=writes, is_dma=True, tag=tag)

    def mm(out_ap, lhsT_ap, rhs_ap, start, stop, reads, writes, tag=""):
        E("tensor",
          lambda e, o=out_ap, l=lhsT_ap, r=rhs_ap, s1=start, s2=stop:
              e.matmul(out=o, lhsT=l, rhs=r, start=s1, stop=s2),
          reads=reads, writes=writes, tag=tag)

    dma("sync", a_sb[:], a4sb[:], ("a_sb",))
    dma("sync", w_sb[:], w4sb[:], ("w_sb",))
    dma("sync", ut_s[:], utri[:], ("ut_s",))
    dma("sync", ones_s[:], onescol[:], ("ones_s",))
    dma("sync", ones1f_s[:], ones1f[:], ("ones1f_s",))
    dma("sync", tlo_s[:], tlocol[:], ("tlo_s",))
    E("gpsimd", lambda e: e.memset(zero_s[:], 0.0), writes=("zero_s",), tag="z0")
    E("gpsimd", lambda e: e.memset(stage[:, 0:4], 0.0), writes=("stage:wt",), tag="z1")
    E("vector", lambda e: e.tensor_copy(out=stage[:, 2:3], in_=tlo_s[:]),
      reads=("tlo_s",), writes=("stage:tlo",), tag="tlo")
    a_bf = sb("a_bf", (128, 256))
    E("vector", lambda e: e.tensor_copy(out=a_bf[:], in_=a_sb[:]),
      reads=("a_sb",), writes=("a_bf",), tag="abf")

    for t in range(64):
        dma("sync", p_out[t * 128:(t + 1) * 128, :], zero_s[:, 0:2048],
            ("P:z%d" % t,), reads=("zero_s",), tag="pz")
    for e in range(4):
        for s in range(8):
            dma("scalar", bufd[e][s * 128:(s + 1) * 128, :], zero_s[:],
                ("bufe%d" % e,), reads=("zero_s",), tag="bz")

    for t in range(64):
        mm(ps_tot[:], ones_s[:], a_bf[:, t * 4:(t + 1) * 4], start=True, stop=True,
           reads=("ones_s", "a_bf"), writes=("ps_tot",), tag="tot_mm")
        E("vector", lambda e, t=t: e.tensor_copy(out=tot_sb[:, t * 4:(t + 1) * 4],
                                                 in_=ps_tot[:]),
          reads=("ps_tot",), writes=("tot_sb:%d" % t,), tag="tot_ev")
    E("gpsimd", lambda e: e.memset(off_sb[0:1, 0:4], 0.0), writes=("off_sb:0",),
      tag="off0")
    for t in range(1, 64):
        E("vector", lambda e, t=t: e.tensor_tensor(
            out=off_sb[0:1, t * 4:(t + 1) * 4],
            in0=off_sb[0:1, (t - 1) * 4: t * 4],
            in1=tot_sb[0:1, (t - 1) * 4: t * 4], op=ALU.add),
          reads=("off_sb:%d" % (t - 1), "tot_sb:%d" % (t - 1)),
          writes=("off_sb:%d" % t,), tag="off_acc")

    for t in range(64):
        mm(ps_pos[:], ut_s[:], a_bf[:, t * 4:(t + 1) * 4], start=True, stop=False,
           reads=("ut_s", "a_bf"), writes=("ps_pos",), tag="pos_mm")
        mm(ps_pos[:], ones1f_s[:], off_sb[0:1, t * 4:(t + 1) * 4],
           start=False, stop=True,
           reads=("ones1f_s", "off_sb:%d" % t), writes=("ps_pos",), tag="pos_bc")
        E("vector", lambda e: e.tensor_copy(out=pos_sb[:], in_=ps_pos[:]),
          reads=("ps_pos",), writes=("pos_sb",), tag="pos_ev")
        E("vector", lambda e, t=t: e.scalar_tensor_tensor(
            out=ind_f[:], in0=a_sb[:, t * 4:(t + 1) * 4], scalar=-1.0e6,
            in1=pos_sb[:], op0=ALU.mult, op1=ALU.add),
          reads=("a_sb", "pos_sb"), writes=("ind_f:a",), tag="ind1")
        E("vector", lambda e: e.tensor_scalar(
            out=ind_f[:], in0=ind_f[:], scalar1=1.0e6, scalar2=None, op0=ALU.add),
          reads=("ind_f:a",), writes=("ind_f",), tag="ind2")
        E("gpsimd", lambda e: e.tensor_copy(out=ind_u[:], in_=ind_f[:]),
          reads=("ind_f",), writes=("ind_u",), tag="ind3")
        dma("sync", stage[:, 4:2052], ln2_all[t * 128:(t + 1) * 128, :],
            ("stage:ln2",), tag="st_ln2")
        E("gpsimd", lambda e, t=t: e.memset(stage[:, 1:2], float(t)),
          writes=("stage:hi",), tag="st_hi")
        for e_ in range(4):
            E("vector", lambda e, t=t, e_=e_: e.tensor_copy(
                out=stage[:, 0:1], in_=w_sb[:, t * 4 + e_: t * 4 + e_ + 1]),
              reads=("w_sb",), writes=("stage:w",), tag="st_w")
            def scat(eng, t=t, e_=e_):
                bc = get_reg(eng, CAP - 1)
                return eng.indirect_dma_start(
                    out=bufd[e_][:], out_offset=IOA(ind_u[:, e_:e_ + 1], 0),
                    in_=stage[:], in_offset=None,
                    bounds_check=bc, oob_is_err=False)
            E("gpsimd", scat,
              reads=("ind_u", "stage:ln2", "stage:hi", "stage:w", "stage:tlo",
                     "stage:wt", "bufe%d" % e_),
              writes=("bufe%d:t%d" % (e_, t),), is_dma=True, tag="scat")

    for e_ in range(4):
        ekey = "bufe%d" % e_
        for k in range(16):
            E("scalar", lambda e, e_=e_, k=k: e.dma_start(
                out=buf_fm[:, k * 1024:(k + 1) * 1024],
                in_=bufd[e_][:, 4 + k * 128: 4 + (k + 1) * 128], transpose=True),
              reads=(ekey,) + tuple("bufe%d:t%d" % (e_, t) for t in ()),
              writes=("buf_fm:%d" % k,), is_dma=True, tag="tbuf")
        dma("sync", wt_sb[:].rearrange("p (s c) -> p s c", s=8),
            bufd[e_][:, 0:4].rearrange("(s p) c -> p s c", p=128),
            ("wt_sb",), reads=(ekey,), tag="ld_wt")
        E("vector", lambda e: e.tensor_copy(out=wcolf[:], in_=wt_sb[:].rearrange(
            "p (s c) -> p s c", s=8)[:, :, 0]),
          reads=("wt_sb",), writes=("wcolf",), tag="wcol")
        for s in range(8):
            E("vector", lambda e, s=s: e.scalar_tensor_tensor(
                out=tcol[:, s:s + 1], in0=wt_sb[:, s * 4 + 1: s * 4 + 2],
                scalar=128.0, in1=wt_sb[:, s * 4 + 2: s * 4 + 3],
                op0=ALU.mult, op1=ALU.add),
              reads=("wt_sb",), writes=("tcol:%d" % s,), tag="trec")
            E("vector", lambda e, s=s: e.tensor_scalar(
                out=oind_f[:, s:s + 1], in0=wcolf[:, s:s + 1], scalar1=0.0,
                scalar2=None, op0=ALU.is_gt),
              reads=("wcolf",), writes=("oind_f:v%d" % s,), tag="vmask")
            E("vector", lambda e, s=s: e.scalar_tensor_tensor(
                out=oind_f[:, s:s + 1], in0=oind_f[:, s:s + 1], scalar=-1.0e6,
                in1=tcol[:, s:s + 1], op0=ALU.mult, op1=ALU.add),
              reads=("oind_f:v%d" % s, "tcol:%d" % s), writes=("oind_f:a%d" % s,),
              tag="oind1")
            E("vector", lambda e, s=s: e.tensor_scalar(
                out=oind_f[:, s:s + 1], in0=oind_f[:, s:s + 1], scalar1=1.0e6,
                scalar2=None, op0=ALU.add),
              reads=("oind_f:a%d" % s,), writes=("oind_f:%d" % s,), tag="oind2")
        E("gpsimd", lambda e: e.tensor_copy(out=oind_u[:], in_=oind_f[:]),
          reads=tuple("oind_f:%d" % s for s in range(8)), writes=("oind_u",),
          tag="oindu")
        dma("gpsimd", wd_sb[:].rearrange("p (i n) -> p i n", i=8),
            wd4[e_].rearrange("(i p) n -> p i n", p=128), ("wd_sb",), tag="ld_wd")
        for m in range(16):
            dma("gpsimd", wgm[:].rearrange("p (k n) -> p k n", k=16),
                wg4[e_, m].rearrange("(k p) n -> p k n", p=128), ("wgm",),
                tag="ld_wg")
            for half in range(2):
                ps = ps_gu if half == 0 else ps_gu2
                pk = "ps_gu" if half == 0 else "ps_gu2"
                for k in range(16):
                    mm(ps[:], wgm[:, k * 128:(k + 1) * 128],
                       buf_fm[:, k * 1024 + half * 512: k * 1024 + (half + 1) * 512],
                       start=(k == 0), stop=(k == 15),
                       reads=("wgm", "buf_fm"), writes=(pk,), tag="gu_mm")
                E("vector", lambda e, m=m, half=half, ps=ps: e.tensor_copy(
                    out=gu_sb[:, m * 1024 + half * 512: m * 1024 + (half + 1) * 512],
                    in_=ps[:]),
                  reads=(pk,), writes=("gu_sb:%d_%d" % (m, half),), tag="gu_ev")
        for i in range(8):
            E("scalar", lambda e, i=i: e.activation(
                out=silu_sb[:], in_=gu_sb[:, i * 1024:(i + 1) * 1024],
                func=AF.Silu),
              reads=("gu_sb:%d_0" % i, "gu_sb:%d_1" % i), writes=("silu_sb",),
              tag="silu")
            E("vector", lambda e, i=i: e.tensor_tensor(
                out=act_fm[:, i * 1024:(i + 1) * 1024], in0=silu_sb[:],
                in1=gu_sb[:, (8 + i) * 1024:(9 + i) * 1024], op=ALU.mult),
              reads=("silu_sb", "gu_sb:%d_0" % (8 + i), "gu_sb:%d_1" % (8 + i)),
              writes=("act_fm:%d" % i,), tag="actm")
        for s in range(8):
            for n in range(4):
                ps = ps_y if n % 2 == 0 else ps_y2
                pk = "ps_y" if n % 2 == 0 else "ps_y2"
                for i in range(8):
                    mm(ps[:], act_fm[:, i * 1024 + s * 128: i * 1024 + (s + 1) * 128],
                       wd_sb[:, i * 2048 + n * 512: i * 2048 + (n + 1) * 512],
                       start=(i == 0), stop=(i == 7),
                       reads=("act_fm", "wd_sb"), writes=(pk,), tag="y_mm")
                E("vector", lambda e, s=s, n=n, ps=ps: e.tensor_scalar(
                    out=yw_sb[:, n * 512:(n + 1) * 512], in0=ps[:],
                    scalar1=wcolf[:, s:s + 1], scalar2=None, op0=ALU.mult),
                  reads=(pk, "wcolf"), writes=("yw_sb:%d" % n,), tag="yw_ev")
            def scat_out(eng, s=s):
                bc = get_reg(eng, 8191)
                return eng.indirect_dma_start(
                    out=p_out[:], out_offset=IOA(oind_u[:, s:s + 1], 0),
                    in_=yw_sb[:], in_offset=None,
                    bounds_check=bc, oob_is_err=False, compute_op=ALU.add)
            E("gpsimd", scat_out,
              reads=("oind_u", "yw_sb:0", "yw_sb:1", "yw_sb:2", "yw_sb:3")
              + tuple("P:z%d" % t for t in range(0)),
              writes=("P",), is_dma=True, tag="scat_y")

    p.build(es)
    es.close()
    return nc


def _l2_inputs(core, ln2_all, route_all, w_gate_up, w_down):
    e0 = 4 * core
    A = route_all[:, e0:e0 + 4]
    W = route_all[:, 32 + e0:32 + e0 + 4]
    a4sb = np.ascontiguousarray(A.reshape(64, 128, 4).transpose(1, 0, 2)).reshape(128, 256)
    w4sb = np.ascontiguousarray(W.reshape(64, 128, 4).transpose(1, 0, 2)).reshape(128, 256)
    wg = w_gate_up[e0:e0 + 4]
    wg4 = np.ascontiguousarray(
        wg.reshape(4, D, 16, 128).transpose(0, 2, 1, 3))
    return {
        "ln2_all": ln2_all,
        "a4sb": a4sb, "w4sb": w4sb,
        "wg4": wg4,
        "wd4": np.ascontiguousarray(w_down[e0:e0 + 4]),
        "utri": np.triu(np.ones((128, 128), np.float32), 1).astype(BF),
        "onescol": np.ones((128, 1), np.float32).astype(BF),
        "ones1f": np.ones((1, 128), np.float32),
        "tlocol": np.arange(128, dtype=np.float32)[:, None].astype(BF),
    }



def build_l3():
    nc = bass.Bass("TRN2", target_bir_lowering=False)

    def din(name, shape, dt=F32):
        return nc.dram_tensor(name, list(shape), dt, kind="ExternalInput").ap()

    pslices = din("pslices", (8, QTOK, D), BF16)
    x1c = din("x1c", (QTOK, D))
    wzc = din("wzc", (QTOK, 1))
    out_c = nc.dram_tensor("out_c", [QTOK, D], F32, kind="ExternalOutput").ap()

    es = ExitStack()
    p = Prog(nc)
    E = p.emit
    cat = es.enter_context(nc.sbuf_tensor("cat", [128, 8 * D], BF16))
    red = es.enter_context(nc.sbuf_tensor("red", [128, D], mybir.dt.float32))
    x1t = es.enter_context(nc.sbuf_tensor("x1t", [128, D], mybir.dt.float32))
    wz1 = es.enter_context(nc.sbuf_tensor("wz1", [128, 1], mybir.dt.float32))
    outt = es.enter_context(nc.sbuf_tensor("outt", [128, D], mybir.dt.float32))

    for t in range(8):
        E("sync", lambda e, t=t: e.dma_start(
            out=cat[:].rearrange("p (j n) -> p j n", j=8),
            in_=pslices[:, t * 128:(t + 1) * 128, :].rearrange("j p n -> p j n")),
          writes=("cat",), is_dma=True, tag="ld_cat")
        E("sync", lambda e, t=t: e.dma_start(
            out=x1t[:], in_=x1c[t * 128:(t + 1) * 128, :]),
          writes=("x1t",), is_dma=True, tag="ld_x1")
        E("sync", lambda e, t=t: e.dma_start(
            out=wz1[:], in_=wzc[t * 128:(t + 1) * 128, :]),
          writes=("wz1",), is_dma=True, tag="ld_wz")
        E("vector", lambda e: e.tensor_reduce(
            out=red[:], in_=cat[:].rearrange("p (j n) -> p n j", j=8),
            axis=AX.X, op=ALU.add),
          reads=("cat",), writes=("red",), tag="red")
        E("vector", lambda e: e.tensor_scalar(
            out=wz1[:], in0=wz1[:], scalar1=1.0, scalar2=None, op0=ALU.add),
          reads=("wz1",), writes=("wz1:p",), tag="wzp")
        E("vector", lambda e: e.scalar_tensor_tensor(
            out=outt[:], in0=red[:], scalar=wz1[:, 0:1], in1=x1t[:],
            op0=ALU.mult, op1=ALU.add),
          reads=("red", "wz1:p", "x1t"), writes=("outt",), tag="fin")
        E("sync", lambda e, t=t: e.dma_start(
            out=out_c[t * 128:(t + 1) * 128, :], in_=outt[:]),
          reads=("outt",), writes=("out_c:%d" % t,), is_dma=True, tag="st_out")

    p.build(es)
    es.close()
    return nc



_NC_CACHE = {}


def _get_progs():
    if "l1" not in _NC_CACHE:
        _NC_CACHE["l1"] = build_l1()
        _NC_CACHE["l2"] = build_l2()
        _NC_CACHE["l3"] = build_l3()
    return _NC_CACHE["l1"], _NC_CACHE["l2"], _NC_CACHE["l3"]


def kernel(L0_attn_x, L0_attn_w_q, L0_attn_w_dkv, L0_attn_w_uk, L0_attn_w_uv,
           L0_attn_w_q_pos, L0_attn_w_k_pos, L0_attn_w_o,
           L0_ffn_w_cls, L0_ffn_bias, L0_ffn_w_gate_up, L0_ffn_w_down):
    x = np.ascontiguousarray(np.asarray(L0_attn_x, np.float32))
    nc1, nc2, nc3 = _get_progs()

    in1 = [_l1_inputs(c, x, L0_attn_w_q, L0_attn_w_dkv, L0_attn_w_uk,
                      L0_attn_w_uv, L0_attn_w_q_pos, L0_attn_w_k_pos,
                      L0_attn_w_o, L0_ffn_w_cls, L0_ffn_bias)
           for c in range(NCORES)]
    r1 = run_bass_kernel_spmd(nc1, in1, core_ids=list(range(NCORES)))
    x1_all = np.concatenate([np.asarray(r["x1_out"], np.float32)
                             for r in r1.results], 0)
    ln2_all = np.concatenate([np.asarray(r["ln2_out"]) for r in r1.results], 0)
    route_all = np.concatenate([np.asarray(r["route_out"], np.float32)
                                for r in r1.results], 0)

    in2 = [_l2_inputs(c, ln2_all, route_all, L0_ffn_w_gate_up, L0_ffn_w_down)
           for c in range(NCORES)]
    r2 = run_bass_kernel_spmd(nc2, in2, core_ids=list(range(NCORES)))
    partials = [np.asarray(r["p_out"]) for r in r2.results]

    in3 = []
    for c in range(NCORES):
        sl = slice(c * QTOK, (c + 1) * QTOK)
        in3.append({
            "pslices": np.ascontiguousarray(
                np.stack([partials[j][sl] for j in range(NCORES)], 0)),
            "x1c": np.ascontiguousarray(x1_all[sl]),
            "wzc": np.ascontiguousarray(route_all[sl, 64:65]),
        })
    r3 = run_bass_kernel_spmd(nc3, in3, core_ids=list(range(NCORES)))
    out = np.concatenate([np.asarray(r["out_c"], np.float32)
                          for r in r3.results], 0)
    return out.reshape(B, T, D)

